# version 25
# baseline (speedup 1.0000x reference)
import contextlib
import numpy as np
import ml_dtypes
import concourse.bass as bass
import concourse.mybir as mybir
from concourse.bass_utils import run_bass_kernel_spmd

F32 = mybir.dt.float32
BF16 = mybir.dt.bfloat16
AF = mybir.ActivationFunctionType
ALU = mybir.AluOpType
AX = mybir.AxisListType
NB = ml_dtypes.bfloat16

R = 2048
D = 4096
NBK = 8
NTC = 16
NSB = 2
TCSB = 8
NOB = 8
NDC = 32
EPS = 1e-5


def rho_init():
    r = np.zeros((2, 128), np.int32)
    for u in range(2):
        for m in range(128):
            r[u, m] = 2 * m + u
    return r


def rho_step(r):
    new = np.zeros_like(r)
    for u2 in range(2):
        for m2 in range(128):
            u_in = m2 // 64
            m_in = (m2 % 64) * 2 + u2
            new[u2, m2] = r[u_in, m_in]
    return new


RHO4 = rho_init()
for _ in range(4):
    RHO4 = rho_step(RHO4)
PERM = np.concatenate([256 * b + RHO4[u4]
                       for b in range(NBK) for u4 in range(2)])


def host_constants():
    k1 = np.arange(64)
    p = np.arange(64)
    w = np.arange(64)[None, :, None]
    k2 = np.arange(64)[None, None, :]
    c = np.arange(64)[:, None, None]
    C = {}
    for name, sig, scale in [("f", -1.0, 1.0), ("v", +1.0, 1.0 / 4096.0)]:
        ang_in = sig * 2 * np.pi * np.outer(p, k1) / 64.0
        C["Ein%sr" % name] = np.vstack([np.cos(ang_in)] * 2).astype(NB)
        C["Ein%si" % name] = np.vstack([np.sin(ang_in)] * 2).astype(NB)
        C["Ein%sn" % name] = np.vstack([-np.sin(ang_in)] * 2).astype(NB)
        ang = sig * 2 * np.pi * (w * c / 4096.0 + w * k2 / 64.0)
        C["M%sR" % name] = np.concatenate(
            [(np.cos(ang) * scale).transpose(1, 0, 2)] * 2, 0).astype(NB)
        C["M%sI" % name] = np.concatenate(
            [(np.sin(ang) * scale).transpose(1, 0, 2)] * 2, 0).astype(NB)
        C["M%sN" % name] = np.concatenate(
            [(-np.sin(ang) * scale).transpose(1, 0, 2)] * 2, 0).astype(NB)
    e2 = np.zeros((128, 2), NB); e2[0, 0] = 1; e2[64, 1] = 1
    hs = np.zeros((128, 2), NB); hs[0:64, 0] = 1; hs[64:128, 1] = 1
    selu = np.zeros((2, 128), NB); selu[0, 0:64] = 1; selu[1, 64:128] = 1
    C["e2"] = e2; C["hs"] = hs; C["selu"] = selu
    return C


def build_nc():
    nc = bass.Bass("TRN2", target_bir_lowering=False,
                   detect_race_conditions=False)

    xcm = nc.dram_tensor("xcm", [128, NBK * 8192], F32, kind="ExternalInput")
    xtm = nc.dram_tensor("xtm", [R, D], F32, kind="ExternalInput")
    wbT = nc.dram_tensor("wbT", [D, D], F32, kind="ExternalInput")
    bb = nc.dram_tensor("bb", [D], F32, kind="ExternalInput")
    wrT = nc.dram_tensor("wrT", [D, 8], F32, kind="ExternalInput")
    acatT = nc.dram_tensor("acatT", [D, 128], F32, kind="ExternalInput")
    bcat = nc.dram_tensor("bcat", [128, D], F32, kind="ExternalInput")
    identb = nc.dram_tensor("identb", [128, 128], BF16, kind="ExternalInput")
    identf = nc.dram_tensor("identf", [128, 128], F32, kind="ExternalInput")
    eopat = nc.dram_tensor("eopat", [8, 128], BF16, kind="ExternalInput")
    ones1 = nc.dram_tensor("ones1", [1, 128], BF16, kind="ExternalInput")
    consts_h = host_constants()
    cdram = {k: nc.dram_tensor("c_" + k, list(v.shape), BF16,
                               kind="ExternalInput")
             for k, v in consts_h.items()}
    out = nc.dram_tensor("out", [R, D], F32, kind="ExternalOutput")
    ffnout = nc.dram_tensor("ffnout", [R, D], BF16)
    y1hbm = nc.dram_tensor("y1hbm", [128, NDC, R], BF16)
    xspill = nc.dram_tensor("xspill", [2, 128, 8192], BF16)

    stp = contextlib.ExitStack()
    sbp = lambda n, s, d: stp.enter_context(nc.sbuf_tensor(n, s, d))

    C = {k: sbp("s_" + k, list(v.shape), BF16) for k, v in consts_h.items()
         if k != "MvI"}

    S = {}
    for name in ["c_ld", "c_ld2", "att", "atv", "atx", "ats", "svh",
                 "x", "v2s", "y1tm", "fmh", "sxw",
                 "wt", "wfree", "ps", "stg", "ffst", "y1l",
                 "lgmm", "hmm", "lgev", "ltT", "exp1", "exp2",
                 "wrtv", "wrtT", "wTev", "wbc", "hp",
                 "ln2b", "nrm", "nmrs", "rl2", "nrmd", "outd"]:
        S[name] = nc.alloc_semaphore("S_" + name)
    sem_nums = sorted(h.num for h in S.values())
    assert sem_nums[-1] - sem_nums[0] + 1 == len(sem_nums)

    with nc.Block() as blk:
        @blk.gpsimd
        def _(g):
            g.sem_clear(range(sem_nums[0], sem_nums[-1] + 1))

    with nc.Block() as blk:
        @blk.sync
        def _(sy):
            for k in sorted(C):
                sy.dma_start(C[k][:], cdram[k].ap()).then_inc(S["c_ld"], 16)
            sy.wait_ge(S["c_ld"], 16 * len(C))

    with contextlib.ExitStack() as p1:
        p1s = lambda n, s, d: p1.enter_context(nc.sbuf_tensor(n, s, d))
        PL = 8192
        xr = p1s("xr", [128, PL], BF16)
        arp = p1s("arp", [128, PL], BF16)
        aip = p1s("aip", [128, PL], BF16)
        brp = p1s("brp", [128, PL], BF16)
        bip = p1s("bip", [128, PL], BF16)
        wrp = p1s("wrp", [128, PL], BF16)
        wip = p1s("wip", [128, PL], BF16)
        ctb = xr
        cttm = p1s("cttm", [128, 4096], BF16)
        s1 = p1s("s1", [128, 128], F32)
        s1b = p1s("s1b", [128, 128], BF16)
        c0e = p1s("c0e", [2, 128], BF16)
        zE = p1s("zE", [2, 128], BF16)
        zR = p1s("zR", [2, 128], BF16)
        zRf = p1s("zRf", [2, 128], F32)
        c0bc = p1s("c0bc", [128, 128], BF16)
        zbc = p1s("zbc", [128, 128], BF16)
        xtb = p1s("xtb", [128, 1, D], F32)
        vb = xtb[:, 0]
        y1tm = p1s("y1tm", [128, 1, D], BF16)
        y1f = p1s("y1f", [128, 1, NDC, 128], BF16)
        bn6 = p1s("bn6", [128, 48], F32)
        bn2 = p1s("bn2", [128, 2], F32)
        vtmp = p1s("vtmp", [128, 1], F32)
        vrec = p1s("vrec", [128, 1], F32)
        rsb = p1s("rsb", [128, 2], F32)
        ps = p1.enter_context(nc.psum_tensor("ps_a", [128, 8, 512], F32))

        ops = []
        cnt = {"att": 0, "atv": 0, "atx": 0, "ats": 0, "svh": 0,
               "x": 0, "v2s": 0, "nrm": 0, "y1tm": 0, "fmh": 0}

        def op(eng, fn):
            ops.append((eng, fn))

        def t_inc():
            cnt["att"] += 1

        def g_load(fn):
            nx = cnt["atx"]
            op("g", lambda g, nx=nx: g.wait_ge(S["atx"], 16 * nx))
            op("g", fn)
            cnt["ats"] += 1
            nl = cnt["ats"]
            op("g", lambda g, nl=nl: g.wait_ge(S["ats"], 16 * nl))

        def y_wait_loads():
            nl = cnt["ats"]
            nx2 = cnt["x"]
            op("y", lambda sy, nl=nl: sy.wait_ge(S["ats"], 16 * nl))
            op("y", lambda sy, nx2=nx2: sy.wait_ge(S["x"], 16 * nx2))

        def inner(set_, real_in, sr, si, dr, di):
            Er = C["Ein%sr" % set_]
            Ei = C["Ein%si" % set_]
            En = C["Ein%sn" % set_]
            evac_done = {}
            E0 = cnt["atv"]
            for n in range(16):
                sl = slice(n * 512, (n + 1) * 512)
                b = 2 * (n % 4)
                if n >= 4:
                    need = evac_done[n - 4]
                    op("t", lambda t, need=need: t.wait_ge(S["atv"], need))
                for u in (0, 1):
                    pp = slice(64 * u, 64 * u + 64)
                    tp = (64 * u, 64 * u)
                    last = (u == 1)
                    if real_in:
                        op("t", lambda t, pp=pp, tp=tp, b=b, sl=sl, sr=sr,
                           Er=Er:
                           t.matmul(ps[pp, b, :], Er[pp], sr[pp, sl],
                                    start=True, stop=True, tile_position=tp))
                        op("t", lambda t, pp=pp, tp=tp, b=b, sl=sl, sr=sr,
                           Ei=Ei, last=last:
                           t.matmul(ps[pp, b + 1, :], Ei[pp], sr[pp, sl],
                                    start=True, stop=True, tile_position=tp
                                    ).then_inc(S["att"], 1) if last else
                           t.matmul(ps[pp, b + 1, :], Ei[pp], sr[pp, sl],
                                    start=True, stop=True, tile_position=tp))
                    else:
                        op("t", lambda t, pp=pp, tp=tp, b=b, sl=sl, sr=sr,
                           Er=Er:
                           t.matmul(ps[pp, b, :], Er[pp], sr[pp, sl],
                                    start=True, stop=False, tile_position=tp))
                        op("t", lambda t, pp=pp, tp=tp, b=b, sl=sl, si=si,
                           En=En:
                           t.matmul(ps[pp, b, :], En[pp], si[pp, sl],
                                    start=False, stop=True, tile_position=tp))
                        op("t", lambda t, pp=pp, tp=tp, b=b, sl=sl, sr=sr,
                           Ei=Ei:
                           t.matmul(ps[pp, b + 1, :], Ei[pp], sr[pp, sl],
                                    start=True, stop=False, tile_position=tp))
                        op("t", lambda t, pp=pp, tp=tp, b=b, sl=sl, si=si,
                           Er=Er, last=last:
                           t.matmul(ps[pp, b + 1, :], Er[pp], si[pp, sl],
                                    start=False, stop=True, tile_position=tp
                                    ).then_inc(S["att"], 1) if last else
                           t.matmul(ps[pp, b + 1, :], Er[pp], si[pp, sl],
                                    start=False, stop=True, tile_position=tp))
                t_inc()
                nmm = cnt["att"]
                op("v", lambda v, nmm=nmm: v.wait_ge(S["att"], nmm))
                op("v", lambda v, b=b, sl=sl, dr=dr:
                   v.tensor_copy(dr[:, sl], ps[:, b, :]))
                op("s", lambda s, nmm=nmm: s.wait_ge(S["att"], nmm))
                op("s", lambda s, b=b, sl=sl, di=di:
                   s.copy(di[:, sl], ps[:, b + 1, :]))
                if n % 2 == 1 or n == 15:
                    k2 = 2 if (n % 2 == 1) else 1
                    op("v", lambda v, k2=k2: v.drain().then_inc(S["atv"], k2))
                    op("s", lambda s, k2=k2: s.drain().then_inc(S["atv"], k2))
                    cnt["atv"] += 2 * k2
                    for j in range(max(0, n - 1), n + 1):
                        evac_done[j] = cnt["atv"]
        def xbar_pl(src, dst):
            nev = cnt["atv"]
            op("y", lambda sy, nev=nev: sy.wait_ge(S["atv"], nev))
            y_wait_loads()
            for s4 in range(4):
                op("y", lambda sy, s4=s4, src=src, dst=dst: sy.dma_start(
                    dst[:].rearrange("a (t s q) -> a t s q", s=4, q=32)
                    [:, :, s4, :],
                    src[32 * s4:32 * (s4 + 1), :],
                    transpose=True).then_inc(S["atx"], 16))
            cnt["atx"] += 4
            nxb = cnt["atx"]
            op("y", lambda sy, nxb=nxb: sy.wait_ge(S["atx"], 16 * nxb))

        def outer(set_, re_out, sr, si, dr, di, special=False):
            MR = C["M%sR" % set_]
            MI = C.get("M%sI" % set_)
            MN = C["M%sN" % set_]
            nxb = cnt["atx"]
            op("t", lambda t, nxb=nxb: t.wait_ge(S["atx"], 16 * nxb))
            evac_done = {}
            for c in range(64):
                b0 = 2 * (c % 4)
                if c >= 4:
                    need = evac_done[c - 4]
                    op("t", lambda t, need=need: t.wait_ge(S["atv"], need))
                cs, cq = c // 32, c % 32
                rr = sr[:].rearrange("a (t s q) -> a t s q", s=4, q=32)
                ri = si[:].rearrange("a (t s q) -> a t s q", s=4, q=32)
                rhs_r = rr[:, :, cs::2, cq]
                rhs_i = ri[:, :, cs::2, cq]
                for u in (0, 1):
                    pp = slice(64 * u, 64 * u + 64)
                    tp = (64 * u, 64 * u)
                    op("t", lambda t, pp=pp, tp=tp, b0=b0, c=c, rhs_r=rhs_r,
                       MR=MR:
                       t.matmul(ps[pp, b0, 0:128], MR[pp, c], rhs_r[pp],
                                start=True, stop=False, tile_position=tp))
                    op("t", lambda t, pp=pp, tp=tp, b0=b0, c=c, rhs_i=rhs_i,
                       MN=MN, last=(re_out and u == 1):
                       t.matmul(ps[pp, b0, 0:128], MN[pp, c], rhs_i[pp],
                                start=False, stop=True, tile_position=tp
                                ).then_inc(S["att"], 1) if last else
                       t.matmul(ps[pp, b0, 0:128], MN[pp, c], rhs_i[pp],
                                start=False, stop=True, tile_position=tp))
                    if not re_out:
                        op("t", lambda t, pp=pp, tp=tp, b0=b0, c=c,
                           rhs_r=rhs_r, MI=MI:
                           t.matmul(ps[pp, b0 + 1, 0:128], MI[pp, c],
                                    rhs_r[pp], start=True, stop=False,
                                    tile_position=tp))
                        op("t", lambda t, pp=pp, tp=tp, b0=b0, c=c,
                           rhs_i=rhs_i, MR=MR, last=(u == 1):
                           t.matmul(ps[pp, b0 + 1, 0:128], MR[pp, c],
                                    rhs_i[pp], start=False, stop=True,
                                    tile_position=tp
                                    ).then_inc(S["att"], 1) if last else
                           t.matmul(ps[pp, b0 + 1, 0:128], MR[pp, c],
                                    rhs_i[pp], start=False, stop=True,
                                    tile_position=tp))
                t_inc()
                if c % 2 == 1:
                    nmm = cnt["att"]
                    b0e = 2 * ((c - 1) % 4)
                    pvr = ps[:, b0e:b0e + 3:2, 0:128].rearrange(
                        "a b (t ui) -> a b t ui", ui=2)
                    if special:
                        dvv = dr[:].rearrange("a (c ui t) -> a c t ui",
                                              c=64, ui=2)[:, c - 1:c + 1]
                    else:
                        dvv = dr[:].rearrange("a (ui t c) -> a c t ui",
                                              ui=2, c=64)[:, c - 1:c + 1]
                    if re_out and ((c // 2) % 2 == 1):
                        op("s", lambda s, nmm=nmm: s.wait_ge(S["att"], nmm))
                        op("s", lambda s, dvv=dvv, pvr=pvr:
                           s.copy(dvv, pvr))
                        cnt["svh"] += 1
                        op("s", lambda s: s.drain().then_inc(S["svh"], 1))
                        op("v", lambda v, k=cnt["svh"]:
                           v.wait_ge(S["svh"], k))
                        op("v", lambda v: v.nop().then_inc(S["atv"], 1))
                        cnt["atv"] += 1
                    else:
                        op("v", lambda v, nmm=nmm: v.wait_ge(S["att"], nmm))
                        op("v", lambda v, dvv=dvv, pvr=pvr:
                           v.tensor_copy(dvv, pvr))
                        op("v", lambda v: v.drain().then_inc(S["atv"], 1))
                        cnt["atv"] += 1
                    if not re_out:
                        pvi = ps[:, b0e + 1:b0e + 4:2, 0:128].rearrange(
                            "a b (t ui) -> a b t ui", ui=2)
                        dvi = di[:].rearrange("a (ui t c) -> a c t ui",
                                              ui=2, c=64)[:, c - 1:c + 1]
                        op("s", lambda s, nmm=nmm: s.wait_ge(S["att"], nmm))
                        op("s", lambda s, dvi=dvi, pvi=pvi:
                           s.copy(dvi, pvi))
                        op("s", lambda s: s.drain().then_inc(S["atv"], 1))
                        cnt["atv"] += 1
                    evac_done[c - 1] = cnt["atv"]
                    evac_done[c] = cnt["atv"]
        def vdrain_inc():
            op("v", lambda v: v.drain().then_inc(S["atv"], 1))
            cnt["atv"] += 1

        for bk in range(NBK):
            g_load(lambda g, bk=bk: g.dma_start(
                xr[:], xcm.ap()[:, bk * 8192:(bk + 1) * 8192]
            ).then_inc(S["ats"], 16))
            nld = cnt["ats"]
            op("t", lambda t, nld=nld: t.wait_ge(S["ats"], 16 * nld))
            inner("f", True, xr, None, arp, aip)
            xbar_pl(arp, brp)
            xbar_pl(aip, bip)
            outer("f", False, brp, bip, wrp, wip)
            nevS = cnt["atv"]
            op("y", lambda sy, nevS=nevS: sy.wait_ge(S["atv"], nevS))
            op("y", lambda sy: sy.dma_start(xspill.ap()[0], wrp[:]
                                            ).then_inc(S["sxw"], 16))
            op("y", lambda sy: sy.dma_start(xspill.ap()[1], wip[:]
                                            ).then_inc(S["sxw"], 16))
            cnt.setdefault("sxw", 0)
            cnt["sxw"] = cnt.get("sxw", 0) + 2
            nxs = cnt["sxw"]
            op("y", lambda sy, nxs=nxs: sy.wait_ge(S["sxw"], 16 * nxs))
            op("s", lambda s: s.activation(aip[:], wip[:], AF.Square))
            cnt["svh"] += 1
            op("s", lambda s: s.drain().then_inc(S["svh"], 1))
            op("v", lambda v: v.tensor_tensor(arp[:], wrp[:], wrp[:],
                                              ALU.mult))
            op("v", lambda v: v.drain())
            op("v", lambda v, k=cnt["svh"]: v.wait_ge(S["svh"], k))
            op("v", lambda v: v.tensor_tensor(xr[:], arp[:], aip[:],
                                              ALU.add))
            vdrain_inc()
            op("t", lambda t, n=cnt["atv"]: t.wait_ge(S["atv"], n))
            inner("v", True, xr, None, arp, aip)
            xbar_pl(arp, brp)
            xbar_pl(aip, bip)
            outer("v", True, brp, bip, wrp, None)
            op("t", lambda t, n=cnt["atv"]: t.wait_ge(S["atv"], n))
            corr_l0 = wrp[:].rearrange("a (m w) -> a m w", w=64)[:, :, 0]
            op("t", lambda t, corr_l0=corr_l0: t.matmul(
                ps[0:2, 0, 0:128], C["e2"][:], corr_l0,
                start=True, stop=True))
            op("t", lambda t: t.drain().then_inc(S["att"], 1)); t_inc()
            op("v", lambda v, n=cnt["att"]: v.wait_ge(S["att"], n))
            op("v", lambda v: v.tensor_copy(c0e[:], ps[0:2, 0, 0:128]))
            vdrain_inc()
            op("t", lambda t, n=cnt["atv"]: t.wait_ge(S["atv"], n))
            op("t", lambda t: t.matmul(ps[:, 1, 0:128], C["selu"][:],
                                       c0e[:], start=True, stop=True))
            op("t", lambda t: t.drain().then_inc(S["att"], 1)); t_inc()
            op("v", lambda v, n=cnt["att"]: v.wait_ge(S["att"], n))
            op("v", lambda v: v.tensor_copy(c0bc[:], ps[:, 1, 0:128]))
            op("v", lambda v: v.drain())
            op("v", lambda v: v.tensor_tensor(
                arp[:].rearrange("a (m w) -> a m w", w=64),
                wrp[:].rearrange("a (m w) -> a m w", w=64),
                c0bc[:].rearrange("a (m o) -> a m o", o=1
                                  ).broadcast_to((128, 128, 64)),
                ALU.subtract))
            vdrain_inc()
            op("s", lambda s, n=cnt["atv"]: s.wait_ge(S["atv"], n))
            op("s", lambda s: s.activation(xr[:], arp[:], AF.Exp))
            op("s", lambda s: s.drain().then_inc(S["v2s"], 1))
            cnt["v2s"] += 1
            op("v", lambda v, n=cnt["v2s"]: v.wait_ge(S["v2s"], n))
            op("v", lambda v: v.reduce_sum(
                s1[:], xr[:].rearrange("a (m w) -> a m w", w=64),
                axis=AX.X))
            op("v", lambda v: v.drain())
            op("v", lambda v: v.tensor_copy(s1b[:], s1[:]))
            vdrain_inc()
            op("t", lambda t, n=cnt["atv"]: t.wait_ge(S["atv"], n))
            op("t", lambda t: t.matmul(ps[0:2, 2, 0:128], C["hs"][:], s1b[:],
                                       start=True, stop=True))
            op("t", lambda t: t.drain().then_inc(S["att"], 1)); t_inc()
            op("v", lambda v, n=cnt["att"]: v.wait_ge(S["att"], n))
            op("v", lambda v: v.tensor_copy(zE[:], ps[0:2, 2, 0:128]))
            op("v", lambda v: v.drain())
            op("v", lambda v: v.reciprocal(zRf[:], zE[:]))
            op("v", lambda v: v.drain())
            op("v", lambda v: v.tensor_copy(zR[:], zRf[:]))
            vdrain_inc()
            op("t", lambda t, n=cnt["atv"]: t.wait_ge(S["atv"], n))
            op("t", lambda t: t.matmul(ps[:, 3, 0:128], C["selu"][:], zR[:],
                                       start=True, stop=True))
            op("t", lambda t: t.drain().then_inc(S["att"], 1)); t_inc()
            op("v", lambda v, n=cnt["att"]: v.wait_ge(S["att"], n))
            op("v", lambda v: v.tensor_copy(zbc[:], ps[:, 3, 0:128]))
            op("v", lambda v: v.drain())
            op("v", lambda v: v.tensor_tensor(
                xr[:].rearrange("a (m w) -> a m w", w=64),
                xr[:].rearrange("a (m w) -> a m w", w=64),
                zbc[:].rearrange("a (m o) -> a m o", o=1
                                 ).broadcast_to((128, 128, 64)),
                ALU.mult))
            op("v", lambda v: v.drain())
            wl0a = xr[:].rearrange("a (m w) -> a m w", w=64)[0:1, :, 0]
            wl0b = xr[:].rearrange("a (m w) -> a m w", w=64)[64:65, :, 0]
            op("v", lambda v, wl0a=wl0a: v.tensor_scalar_add(wl0a, wl0a, -1.0))
            op("v", lambda v, wl0b=wl0b: v.tensor_scalar_add(wl0b, wl0b, -1.0))
            vdrain_inc()
            op("t", lambda t, n=cnt["atv"]: t.wait_ge(S["atv"], n))
            inner("f", True, xr, None, arp, aip)
            xbar_pl(arp, brp)
            xbar_pl(aip, bip)
            outer("f", False, brp, bip, wrp, wip)
            nevW = cnt["atv"]
            op("g", lambda g, nevW=nevW: g.wait_ge(S["atv"], nevW))
            g_load(lambda g: g.dma_start(arp[:], xspill.ap()[0]
                                         ).then_inc(S["ats"], 16))
            g_load(lambda g: g.dma_start(aip[:], xspill.ap()[1]
                                         ).then_inc(S["ats"], 16))
            nXr = cnt["ats"]
            op("v", lambda v, nXr=nXr: v.wait_ge(S["ats"], 16 * nXr))
            op("v", lambda v: v.tensor_tensor(brp[:], arp[:], wrp[:],
                                              ALU.mult))
            op("v", lambda v: v.tensor_tensor(bip[:], aip[:], wip[:],
                                              ALU.mult))
            op("v", lambda v: v.drain())
            op("v", lambda v: v.tensor_tensor(brp[:], brp[:], bip[:],
                                              ALU.add))
            op("v", lambda v: v.tensor_tensor(bip[:], aip[:], wrp[:],
                                              ALU.mult))
            op("v", lambda v: v.drain())
            op("v", lambda v: v.tensor_tensor(arp[:], arp[:], wip[:],
                                              ALU.mult))
            op("v", lambda v: v.drain())
            op("v", lambda v: v.tensor_tensor(bip[:], bip[:], arp[:],
                                              ALU.subtract))
            vdrain_inc()
            op("t", lambda t, n=cnt["atv"]: t.wait_ge(S["atv"], n))
            inner("v", False, brp, bip, arp, aip)
            xbar_pl(arp, wrp)
            xbar_pl(aip, wip)
            outer("v", True, wrp, wip, ctb, None, special=True)
            nevC = cnt["atv"]
            op("y", lambda sy, nevC=nevC: sy.wait_ge(S["atv"], nevC))

            for u4 in range(2):
                tc = bk * 2 + u4
                for st2 in range(2):
                    op("y", lambda sy, u4=u4, st2=st2: sy.dma_start(
                        cttm[:].rearrange("a (c iq) -> a c iq", iq=64)
                        [:, :, st2 * 32:(st2 + 1) * 32],
                        ctb[64 * u4 + 32 * st2:
                            64 * u4 + 32 * (st2 + 1), :],
                        transpose=True).then_inc(S["atx"], 16))
                    cnt["atx"] += 1
                nxc = cnt["atx"]
                op("y", lambda sy, nxc=nxc: sy.wait_ge(S["atx"], 16 * nxc))
                g_load(lambda g, tc=tc: g.dma_start(
                    xtb[:, 0], xtm.ap()[tc * 128:(tc + 1) * 128, :]
                ).then_inc(S["ats"], 16))
                nx = cnt["ats"]
                op("v", lambda v, nx=nx: v.wait_ge(S["ats"], 16 * nx))
                nxc2 = cnt["atx"]
                op("v", lambda v, nxc2=nxc2: v.wait_ge(S["atx"], 16 * nxc2))
                op("v", lambda v, tc=tc, u4=u4: v.scalar_tensor_tensor(
                    vb.rearrange("a (i2 c) -> a i2 c", i2=64),
                    xtb[:, 0].rearrange("a (i2 c) -> a i2 c", i2=64),
                    2.0,
                    cttm[:].rearrange("a (c i2) -> a c i2", i2=64
                                      ).transpose([0, 2, 1]),
                    ALU.mult, ALU.add))
                op("v", lambda v: v.drain())
                for k in range(8):
                    op("v", lambda v, k=k: v.bn_stats(
                        bn6[:, 6 * k:6 * k + 6],
                        vb[:, 512 * k:512 * (k + 1)]))
                op("v", lambda v: v.drain())
                op("v", lambda v: v.bn_aggr(bn2[:], bn6[:]))
                op("v", lambda v: v.drain())
                op("v", lambda v: v.tensor_scalar_add(vtmp[:], bn2[:, 1:2],
                                                      EPS))
                op("v", lambda v: v.drain().then_inc(S["v2s"], 1))
                cnt["v2s"] += 1
                nv = cnt["v2s"]
                op("s", lambda s, nv=nv: s.wait_ge(S["v2s"], nv))
                op("s", lambda s: s.activation(vrec[:], vtmp[:], AF.Ln))
                op("s", lambda s: s.drain())
                op("s", lambda s, tc=tc: s.activation(
                    rsb[:, tc % 2:tc % 2 + 1], vrec[:], AF.Exp,
                    scale=-0.5))
                op("s", lambda s: s.drain().then_inc(S["nrm"], 1))
                cnt["nrm"] += 1
                nn = cnt["nrm"]
                op("v", lambda v, nn=nn: v.wait_ge(S["nrm"], nn))
                nxw = cnt["atx"]
                op("v", lambda v, nxw=nxw: v.wait_ge(S["atx"], 16 * nxw))
                op("v", lambda v, tc=tc: v.tensor_scalar(
                    y1tm[:, 0], vb, bn2[:, 0:1],
                    rsb[:, tc % 2:tc % 2 + 1], ALU.subtract, ALU.mult))
                op("v", lambda v: v.drain().then_inc(S["y1tm"], 1))
                cnt["y1tm"] += 1
                ny = cnt["y1tm"]
                op("y", lambda sy, ny=ny: sy.wait_ge(S["y1tm"], ny))
                y_wait_loads()
                for s4 in range(4):
                    op("y", lambda sy, s4=s4, tc=tc: sy.dma_start(
                        y1f[:, 0, :, 32 * s4:32 * (s4 + 1)],
                        y1tm[32 * s4:32 * (s4 + 1), 0],
                        transpose=True).then_inc(S["atx"], 16))
                    cnt["atx"] += 1
                nfy = cnt["atx"]
                op("y", lambda sy, nfy=nfy: sy.wait_ge(S["atx"], 16 * nfy))
                op("y", lambda sy, tc=tc: sy.dma_start(
                    y1hbm.ap()[:, :, tc * 128:(tc + 1) * 128],
                    y1f[:, 0]).then_inc(S["fmh"], 16))
                cnt["fmh"] += 1
                nfh = cnt["fmh"]
                op("y", lambda sy, nfh=nfh: sy.wait_ge(S["fmh"], 16 * nfh))

        with nc.Block() as blk:
            @blk.tensor
            def _(t):
                for e, fn in ops:
                    if e == "t":
                        fn(t)

            @blk.vector
            def _(v):
                for e, fn in ops:
                    if e == "v":
                        fn(v)

            @blk.scalar
            def _(s):
                for e, fn in ops:
                    if e == "s":
                        fn(s)

            @blk.sync
            def _(sy):
                for e, fn in ops:
                    if e == "y":
                        fn(sy)

            @blk.gpsimd
            def _(g):
                for e, fn in ops:
                    if e == "g":
                        fn(g)

    with contextlib.ExitStack() as p2:
        p2s = lambda n, s, d: p2.enter_context(nc.sbuf_tensor(n, s, d))
        NW = 4
        ident_b = p2s("ident_b", [128, 128], BF16)
        ident_f = p2s("ident_f", [128, 128], F32)
        eop = p2s("eop", [8, 128], BF16)
        one1 = p2s("one1", [1, 128], BF16)
        wr_sb = p2s("wr_sb", [128, NDC, 8], BF16)
        acat_sb = p2s("acat_sb", [128, NDC, 128], BF16)
        bcat_sb = p2s("bcat_sb", [128, D], BF16)
        bb_sb = p2s("bb_sb", [1, D], BF16)
        y1sb = p2s("y1sb", [128, NDC, 1024], BF16)
        wt = p2s("wt", [128, 4, NW * 512], BF16)
        stg = p2s("stg", [128, 4, 512], BF16)
        lg_sb = p2s("lg_sb", [8, 1024], F32)
        ltm = p2s("ltm", [128, TCSB, 8], F32)
        rmx = p2s("rmx", [128, TCSB, 1], F32)
        rtmp = p2s("rtmp", [128, TCSB, 1], F32)
        prb = p2s("prb", [128, TCSB, 8], F32)
        pm1 = p2s("pm1", [128, TCSB, 1], F32)
        pmsk = p2s("pmsk", [128, TCSB, 8], F32)
        ptmp = p2s("ptmp", [128, TCSB, 8], F32)
        pm2 = p2s("pm2", [128, TCSB, 1], F32)
        wrtw = p2s("wrtw", [128, TCSB, 8], F32)
        wT_sb = p2s("wT_sb", [8, 1024], BF16)
        h_sb = p2s("h_sb", [128, 1024], F32)
        hp_sb = p2s("hp_sb", [128, 1024], BF16)
        stats = p2s("stats", [128, TCSB, NOB, 6], F32)
        m2 = p2s("m2", [128, TCSB, 2], F32)
        vrec2 = p2s("vrec2", [128, TCSB], F32)
        lt2 = p2s("lt2", [128, 1], F32)
        rs2 = p2s("rs2", [128, NTC], F32)
        nmrs = p2s("nmrs", [128, NTC], F32)
        rlb = p2s("rlb", [128, 4, 512], BF16)
        outb = p2s("outb", [128, 4, 512], F32)
        psum_a = p2.enter_context(nc.psum_tensor("psum_a", [128, 8, 512],
                                                 F32))
        WDC = NDC // NW

        with nc.Block() as blk:
            @blk.sync
            def _(sy2):
                sy2.dma_start(ident_b[:], identb.ap()).then_inc(S["c_ld"], 16)
                sy2.dma_start(ident_f[:], identf.ap()).then_inc(S["c_ld"], 16)
                sy2.dma_start(eop[:], eopat.ap()).then_inc(S["c_ld"], 16)
                sy2.dma_start(one1[:], ones1.ap()).then_inc(S["c_ld"], 16)

            @blk.gpsimd
            def _(g):
                g.dma_start(wr_sb[:],
                            wrT.ap().rearrange("(dc p) e -> p dc e", p=128)
                            ).then_inc(S["c_ld2"], 16)
                g.dma_start(acat_sb[:],
                            acatT.ap().rearrange("(dc p) e -> p dc e", p=128)
                            ).then_inc(S["c_ld2"], 16)
                g.dma_start(bcat_sb[:], bcat.ap()).then_inc(S["c_ld2"], 16)
                g.dma_start(bb_sb[0:1, :],
                            bb.ap().rearrange("(a d) -> a d", a=1)
                            ).then_inc(S["c_ld2"], 16)
                k = 0
                for sb in range(NSB):
                    if sb > 0:
                        g.wait_ge(S["ps"], NOB * sb)
                    g.dma_start(y1sb[:],
                                y1hbm.ap()[:, :, sb * 1024:(sb + 1) * 1024]
                                ).then_inc(S["y1l"], 16)
                    for ob in range(NOB):
                        for wq in range(WDC):
                            if k >= 4:
                                g.wait_ge(S["wfree"], k - 3)
                            g.dma_start(
                                wt[:, k % 4].rearrange("p (a b) -> p a b",
                                                       a=NW),
                                wbT.ap().rearrange("(dc p) o -> p dc o",
                                                   p=128)
                                [:, wq * NW:(wq + 1) * NW,
                                 ob * 512:(ob + 1) * 512],
                            ).then_inc(S["wt"], 16)
                            k += 1

            @blk.tensor
            def _(t):
                k = 0
                nevac = 0
                for sb in range(NSB):
                    if sb == 0:
                        t.wait_ge(S["c_ld"], 16 * (len(C) + 4))
                        t.wait_ge(S["c_ld2"], 16 * 4)
                    t.wait_ge(S["y1l"], 16 * (sb + 1))
                    if sb > 0:
                        t.wait_ge(S["stg"], 64 * sb)
                    for half in range(2):
                        for dc in range(NDC):
                            t.matmul(psum_a[0:8, half, :], wr_sb[:, dc],
                                     y1sb[:, dc,
                                          512 * half:512 * (half + 1)],
                                     start=(dc == 0), stop=(dc == NDC - 1))
                    t.drain().then_inc(S["lgmm"], 1)
                    for half in range(2):
                        for dc in range(NDC):
                            t.matmul(psum_a[:, 2 + half, :], acat_sb[:, dc],
                                     y1sb[:, dc,
                                          512 * half:512 * (half + 1)],
                                     start=(dc == 0), stop=(dc == NDC - 1))
                    t.drain().then_inc(S["hmm"], 1)
                    t.wait_ge(S["lgev"], sb + 1)
                    for q in range(TCSB):
                        t.transpose(psum_a[:, 4, q * 8:(q + 1) * 8],
                                    lg_sb[:, q * 128:(q + 1) * 128],
                                    ident_f[0:8, 0:8])
                    t.drain().then_inc(S["ltT"], 1)
                    t.wait_ge(S["wrtv"], sb + 1)
                    for q in range(TCSB):
                        t.transpose(psum_a[0:8, 4 + q // 4,
                                           (q % 4) * 128:(q % 4) * 128 + 128],
                                    wrtw[:, q], ident_f[:])
                    t.drain().then_inc(S["wrtT"], 1)
                    t.wait_ge(S["wTev"], sb + 1)
                    for half in range(2):
                        t.matmul(psum_a[:, 6 + half, :], eop[:],
                                 wT_sb[:, 512 * half:512 * (half + 1)],
                                 start=True, stop=True)
                    t.drain().then_inc(S["wbc"], 1)
                    t.wait_ge(S["hp"], sb + 1)
                    for ob in range(NOB):
                        for dc in range(NDC):
                            if dc % NW == 0:
                                t.wait_ge(S["wt"], 16 * (k + 1))
                            wslice = wt[:, k % 4, (dc % NW) * 512:
                                        (dc % NW) * 512 + 512]
                            for q in range(TCSB):
                                if ob >= 1 and dc == 0:
                                    t.wait_ge(S["stg"],
                                              nevac - TCSB + q + 1)
                                last_mm = t.matmul(
                                    psum_a[:, q, :],
                                    y1sb[:, dc, q * 128:(q + 1) * 128],
                                    wslice, start=(dc == 0), stop=False)
                                if ob * 512 <= dc * 128 < (ob + 1) * 512:
                                    off = dc * 128 - ob * 512
                                    t.matmul(psum_a[:, q, off:off + 128],
                                             y1sb[:, dc,
                                                  q * 128:(q + 1) * 128],
                                             ident_b[:], start=False,
                                             stop=False)
                            if dc % NW == NW - 1:
                                last_mm.then_inc(S["wfree"], 1)
                                k += 1
                        for q in range(TCSB):
                            t.matmul(psum_a[:, q, :],
                                     hp_sb[:, q * 128:(q + 1) * 128],
                                     bcat_sb[:, ob * 512:(ob + 1) * 512],
                                     start=False, stop=False)
                            t.matmul(psum_a[:, q, :], one1[:],
                                     bb_sb[:, ob * 512:(ob + 1) * 512],
                                     start=False, stop=True)
                        t.drain().then_inc(S["ps"], 1)
                        nevac += TCSB

            @blk.vector
            def _(v):
                nps = 0
                nevac = 0
                for sb in range(NSB):
                    v.wait_ge(S["lgmm"], sb + 1)
                    v.tensor_copy(lg_sb[:],
                                  psum_a[0:8, 0:2, :].rearrange(
                                      "p a b -> p (a b)"))
                    v.drain().then_inc(S["lgev"], 1)
                    v.wait_ge(S["ltT"], sb + 1)
                    v.tensor_copy(ltm[:],
                                  psum_a[:, 4, 0:TCSB * 8].rearrange(
                                      "p (a b) -> p a b", b=8))
                    v.drain()
                    v.reduce_max(rmx[:], ltm[:], axis=AX.X)
                    v.drain()
                    v.tensor_tensor(prb[:], ltm[:],
                                    rmx[:].broadcast_to((128, TCSB, 8)),
                                    ALU.subtract)
                    v.drain().then_inc(S["exp1"], 1)
                    v.wait_ge(S["exp2"], sb + 1)
                    v.reduce_max(pm1[:], prb[:], axis=AX.X)
                    v.drain()
                    v.tensor_tensor(pmsk[:], prb[:],
                                    pm1[:].broadcast_to((128, TCSB, 8)),
                                    ALU.is_equal)
                    v.drain()
                    v.tensor_tensor(ptmp[:], prb[:], pmsk[:], ALU.mult)
                    v.drain()
                    v.tensor_tensor(ptmp[:], prb[:], ptmp[:], ALU.subtract)
                    v.drain()
                    v.reduce_max(pm2[:], ptmp[:], axis=AX.X)
                    v.drain()
                    v.tensor_tensor(pmsk[:], prb[:],
                                    pm2[:].broadcast_to((128, TCSB, 8)),
                                    ALU.is_ge)
                    v.drain()
                    v.tensor_tensor(wrtw[:], prb[:], pmsk[:], ALU.mult)
                    v.tensor_add(rtmp[:], pm1[:], pm2[:])
                    v.drain()
                    v.reciprocal(pm1[:], rtmp[:])
                    v.drain()
                    v.tensor_scalar_mul(pm1[:], pm1[:], 2.0)
                    v.drain()
                    v.tensor_tensor(wrtw[:], wrtw[:],
                                    pm1[:].broadcast_to((128, TCSB, 8)),
                                    ALU.mult)
                    v.drain().then_inc(S["wrtv"], 1)
                    v.wait_ge(S["wrtT"], sb + 1)
                    v.tensor_copy(wT_sb[:],
                                  psum_a[0:8, 4:6, :].rearrange(
                                      "p a b -> p (a b)"))
                    v.drain().then_inc(S["wTev"], 1)
                    v.wait_ge(S["hmm"], sb + 1)
                    v.wait_ge(S["wbc"], sb + 1)
                    v.tensor_copy(h_sb[:],
                                  psum_a[:, 2:4, :].rearrange(
                                      "p a b -> p (a b)"))
                    v.drain()
                    v.tensor_tensor(hp_sb[:, 0:512], h_sb[:, 0:512],
                                    psum_a[:, 6, :], ALU.mult)
                    v.tensor_tensor(hp_sb[:, 512:1024], h_sb[:, 512:1024],
                                    psum_a[:, 7, :], ALU.mult)
                    v.drain().then_inc(S["hp"], 1)
                    for ob in range(NOB):
                        nps += 1
                        v.wait_ge(S["ps"], nps)
                        for q in range(TCSB):
                            v.bn_stats(stats[:, q, ob, :], psum_a[:, q, :])
                            if nevac >= 4:
                                v.wait_ge(S["ffst"], 16 * (nevac - 3))
                            v.tensor_copy(stg[:, nevac % 4], psum_a[:, q, :])
                            v.drain().then_inc(S["stg"], 1)
                            nevac += 1
                    v.drain()
                    for q in range(TCSB):
                        tc = sb * TCSB + q
                        v.bn_aggr(m2[:, q],
                                  stats[:, q].rearrange("p a b -> p (a b)"))
                        v.drain()
                        v.tensor_scalar_add(vrec2[:, q:q + 1],
                                            m2[:, q, 1:2], EPS)
                        v.drain().then_inc(S["ln2b"], 1)
                        v.wait_ge(S["nmrs"], tc + 1)
                        v.tensor_scalar(nmrs[:, tc:tc + 1], m2[:, q, 0:1],
                                        rs2[:, tc:tc + 1], -1.0,
                                        ALU.mult, ALU.mult)
                        v.drain()
                    for q in range(TCSB):
                        tc = sb * TCSB + q
                        for f in range(8):
                            idx = tc * 8 + f
                            v.wait_ge(S["rl2"],
                                      16 * min(idx + 3, 64 * (sb + 1)))
                            if idx >= 4:
                                v.wait_ge(S["outd"], 16 * (idx - 3))
                            v.tensor_scalar(outb[:, idx % 4],
                                            rlb[:, idx % 4],
                                            rs2[:, tc:tc + 1],
                                            nmrs[:, tc:tc + 1],
                                            ALU.mult, ALU.add)
                            v.drain().then_inc(S["nrmd"], 1)

            @blk.scalar
            def _(s):
                for sb in range(NSB):
                    s.wait_ge(S["exp1"], sb + 1)
                    s.activation(prb[:], prb[:], AF.Exp)
                    s.drain().then_inc(S["exp2"], 1)
                    for q in range(TCSB):
                        tc = sb * TCSB + q
                        s.wait_ge(S["ln2b"], tc + 1)
                        s.activation(lt2[:], vrec2[:, q:q + 1], AF.Ln)
                        s.drain()
                        s.activation(rs2[:, tc:tc + 1], lt2[:],
                                     AF.Exp, scale=-0.5)
                        s.drain().then_inc(S["nmrs"], 1)

            @blk.sync
            def _(sy):
                nst = 0
                for sb in range(NSB):
                    for ob in range(NOB):
                        for q in range(TCSB):
                            tc = sb * TCSB + q
                            sy.wait_ge(S["stg"], nst + 1)
                            sy.dma_start(
                                ffnout.ap()[tc * 128:(tc + 1) * 128,
                                            ob * 512:(ob + 1) * 512],
                                stg[:, nst % 4],
                            ).then_inc(S["ffst"], 16)
                            nst += 1
                    sy.wait_ge(S["ffst"], 16 * 64 * (sb + 1))
                    for q in range(TCSB):
                        tc = sb * TCSB + q
                        for f in range(8):
                            idx = tc * 8 + f
                            if idx >= 4:
                                sy.wait_ge(S["nrmd"], idx - 3)
                            sy.dma_start(
                                rlb[:, idx % 4],
                                ffnout.ap()[tc * 128:(tc + 1) * 128,
                                            f * 512:(f + 1) * 512]
                            ).then_inc(S["rl2"], 16)
                            if idx >= 4:
                                pidx = idx - 4
                                sy.wait_ge(S["nrmd"], pidx + 2)
                                ptc, pf = pidx // 8, pidx % 8
                                sy.dma_start(
                                    out.ap()[ptc * 128:(ptc + 1) * 128,
                                             pf * 512:(pf + 1) * 512],
                                    outb[:, pidx % 4],
                                ).then_inc(S["outd"], 16)
                for pidx in range(NTC * 8 - 4, NTC * 8):
                    sy.wait_ge(S["nrmd"], min(pidx + 2, NTC * 8))
                    ptc, pf = pidx // 8, pidx % 8
                    sy.dma_start(out.ap()[ptc * 128:(ptc + 1) * 128,
                                          pf * 512:(pf + 1) * 512],
                                 outb[:, pidx % 4]).then_inc(S["outd"], 16)
                sy.wait_ge(S["outd"], 16 * NTC * 8)

    stp.close()
    return nc


def kernel(L0_attn_x, L0_ffn_W_base, L0_ffn_b_base, L0_ffn_W_router,
           L0_ffn_lora_A, L0_ffn_lora_B):
    xf = np.ascontiguousarray(
        np.asarray(L0_attn_x, dtype=np.float32).reshape(16384, D))
    wbT_h = np.ascontiguousarray(np.asarray(L0_ffn_W_base, np.float32).T)
    bb_h = np.ascontiguousarray(np.asarray(L0_ffn_b_base, np.float32))
    wrT_h = np.ascontiguousarray(np.asarray(L0_ffn_W_router, np.float32).T)
    acatT_h = np.ascontiguousarray(
        np.asarray(L0_ffn_lora_A, np.float32).reshape(128, D).T)
    bcat_h = np.ascontiguousarray(
        np.asarray(L0_ffn_lora_B, np.float32).transpose(0, 2, 1)
        .reshape(128, D))
    identb_h = np.eye(128, dtype=NB)
    identf_h = np.eye(128, dtype=np.float32)
    eopat_h = (np.arange(128)[None, :] // 16 ==
               np.arange(8)[:, None]).astype(NB)
    ones1_h = np.ones((1, 128), dtype=NB)
    consts_h = host_constants()

    nc = build_nc()
    in_maps = []
    for c in range(8):
        xs = xf[c * R:(c + 1) * R]
        xcm_h = np.ascontiguousarray(
            xs.reshape(NBK, 128, 2, 64, 64).transpose(2, 3, 0, 1, 4)
            .reshape(128, NBK * 8192))
        xtm_h = np.ascontiguousarray(xs[PERM])
        m = dict(xcm=xcm_h, xtm=xtm_h, wbT=wbT_h, bb=bb_h, wrT=wrT_h,
                 acatT=acatT_h, bcat=bcat_h, identb=identb_h,
                 identf=identf_h, eopat=eopat_h, ones1=ones1_h)
        for k, v in consts_h.items():
            m["c_" + k] = v
        in_maps.append(m)
    res = run_bass_kernel_spmd(nc, in_maps, core_ids=list(range(8)))
    outp = np.empty((16384, D), np.float32)
    inv = np.empty(R, np.int64)
    inv[PERM] = np.arange(R)
    for c in range(8):
        outp[c * R:(c + 1) * R] = res.results[c]["out"][inv]
    return outp.reshape(4, 4096, 4096).astype(np.float32)


# version 26 (final)
# speedup vs baseline: 1.1959x; 1.1959x over previous
import contextlib
import numpy as np
import ml_dtypes
import concourse.bass as bass
import concourse.mybir as mybir
from concourse.bass_utils import run_bass_kernel_spmd

F32 = mybir.dt.float32
BF16 = mybir.dt.bfloat16
AF = mybir.ActivationFunctionType
ALU = mybir.AluOpType
AX = mybir.AxisListType
NB = ml_dtypes.bfloat16

R = 2048
D = 4096
NBK = 8
NTC = 16
NSB = 2
TCSB = 8
NOB = 8
NDC = 32
EPS = 1e-5


def rho_init():
    r = np.zeros((2, 128), np.int32)
    for u in range(2):
        for m in range(128):
            r[u, m] = 2 * m + u
    return r


def rho_step(r):
    new = np.zeros_like(r)
    for u2 in range(2):
        for m2 in range(128):
            u_in = m2 // 64
            m_in = (m2 % 64) * 2 + u2
            new[u2, m2] = r[u_in, m_in]
    return new


RHO4 = rho_init()
for _ in range(4):
    RHO4 = rho_step(RHO4)
PERM = np.concatenate([256 * b + RHO4[u4]
                       for b in range(NBK) for u4 in range(2)])


def host_constants():
    k1 = np.arange(64)
    p = np.arange(64)
    w = np.arange(64)[None, :, None]
    k2 = np.arange(64)[None, None, :]
    c = np.arange(64)[:, None, None]
    C = {}
    for name, sig, scale in [("f", -1.0, 1.0), ("v", +1.0, 1.0 / 4096.0)]:
        ang_in = sig * 2 * np.pi * np.outer(p, k1) / 64.0
        C["Ein%sr" % name] = np.vstack([np.cos(ang_in)] * 2).astype(NB)
        C["Ein%si" % name] = np.vstack([np.sin(ang_in)] * 2).astype(NB)
        C["Ein%sn" % name] = np.vstack([-np.sin(ang_in)] * 2).astype(NB)
        ang = sig * 2 * np.pi * (w * c / 4096.0 + w * k2 / 64.0)
        C["M%sR" % name] = np.concatenate(
            [(np.cos(ang) * scale).transpose(1, 0, 2)] * 2, 0).astype(NB)
        C["M%sI" % name] = np.concatenate(
            [(np.sin(ang) * scale).transpose(1, 0, 2)] * 2, 0).astype(NB)
        C["M%sN" % name] = np.concatenate(
            [(-np.sin(ang) * scale).transpose(1, 0, 2)] * 2, 0).astype(NB)
    e2 = np.zeros((128, 2), NB); e2[0, 0] = 1; e2[64, 1] = 1
    hs = np.zeros((128, 2), NB); hs[0:64, 0] = 1; hs[64:128, 1] = 1
    selu = np.zeros((2, 128), NB); selu[0, 0:64] = 1; selu[1, 64:128] = 1
    C["e2"] = e2; C["hs"] = hs; C["selu"] = selu
    return C


def build_nc():
    nc = bass.Bass("TRN2", target_bir_lowering=False,
                   detect_race_conditions=False)

    xcm = nc.dram_tensor("xcm", [128, NBK * 8192], F32, kind="ExternalInput")
    xtm = nc.dram_tensor("xtm", [R, D], F32, kind="ExternalInput")
    wbT = nc.dram_tensor("wbT", [D, D], F32, kind="ExternalInput")
    bb = nc.dram_tensor("bb", [D], F32, kind="ExternalInput")
    wrT = nc.dram_tensor("wrT", [D, 8], F32, kind="ExternalInput")
    acatT = nc.dram_tensor("acatT", [D, 128], F32, kind="ExternalInput")
    bcat = nc.dram_tensor("bcat", [128, D], F32, kind="ExternalInput")
    identb = nc.dram_tensor("identb", [128, 128], BF16, kind="ExternalInput")
    identf = nc.dram_tensor("identf", [128, 128], F32, kind="ExternalInput")
    eopat = nc.dram_tensor("eopat", [8, 128], BF16, kind="ExternalInput")
    ones1 = nc.dram_tensor("ones1", [1, 128], BF16, kind="ExternalInput")
    consts_h = host_constants()
    cdram = {k: nc.dram_tensor("c_" + k, list(v.shape), BF16,
                               kind="ExternalInput")
             for k, v in consts_h.items()}
    out = nc.dram_tensor("out", [R, D], F32, kind="ExternalOutput")
    ffnout = nc.dram_tensor("ffnout", [R, D], BF16)
    y1hbm = nc.dram_tensor("y1hbm", [128, NDC, R], BF16)
    xspill = nc.dram_tensor("xspill", [2, 128, 8192], BF16)

    stp = contextlib.ExitStack()
    sbp = lambda n, s, d: stp.enter_context(nc.sbuf_tensor(n, s, d))

    C = {k: sbp("s_" + k, list(v.shape), BF16) for k, v in consts_h.items()
         if k != "MvI"}

    S = {}
    for name in ["c_ld", "c_ld2", "att", "atv", "atx", "ats", "svh",
                 "x", "v2s", "y1tm", "fmh", "sxw",
                 "wt", "wfree", "ps", "stg", "ffst", "y1l",
                 "lgmm", "hmm", "lgev", "ltT", "exp1", "exp2",
                 "wrtv", "wrtT", "wTev", "wbc", "hp",
                 "ln2b", "nrm", "nmrs", "rl2", "nrmd", "outd"]:
        S[name] = nc.alloc_semaphore("S_" + name)
    sem_nums = sorted(h.num for h in S.values())
    assert sem_nums[-1] - sem_nums[0] + 1 == len(sem_nums)

    with nc.Block() as blk:
        @blk.gpsimd
        def _(g):
            g.sem_clear(range(sem_nums[0], sem_nums[-1] + 1))

    with nc.Block() as blk:
        @blk.sync
        def _(sy):
            for k in sorted(C):
                sy.dma_start(C[k][:], cdram[k].ap()).then_inc(S["c_ld"], 16)
            sy.wait_ge(S["c_ld"], 16 * len(C))

    with contextlib.ExitStack() as p1:
        p1s = lambda n, s, d: p1.enter_context(nc.sbuf_tensor(n, s, d))
        PL = 8192
        xr = p1s("xr", [128, PL], BF16)
        arp = p1s("arp", [128, PL], BF16)
        aip = p1s("aip", [128, PL], BF16)
        brp = p1s("brp", [128, PL], BF16)
        bip = p1s("bip", [128, PL], BF16)
        wrp = p1s("wrp", [128, PL], BF16)
        wip = p1s("wip", [128, PL], BF16)
        ctb = xr
        cttm = p1s("cttm", [128, 4096], BF16)
        s1 = p1s("s1", [128, 128], F32)
        s1b = p1s("s1b", [128, 128], BF16)
        c0e = p1s("c0e", [2, 128], BF16)
        zE = p1s("zE", [2, 128], BF16)
        zR = p1s("zR", [2, 128], BF16)
        zRf = p1s("zRf", [2, 128], F32)
        c0bc = p1s("c0bc", [128, 128], BF16)
        zbc = p1s("zbc", [128, 128], BF16)
        xtb = p1s("xtb", [128, 1, D], F32)
        vb = xtb[:, 0]
        y1tm = p1s("y1tm", [128, 1, D], BF16)
        y1f = p1s("y1f", [128, 1, NDC, 128], BF16)
        bn6 = p1s("bn6", [128, 48], F32)
        bn2 = p1s("bn2", [128, 2], F32)
        vtmp = p1s("vtmp", [128, 1], F32)
        vrec = p1s("vrec", [128, 1], F32)
        rsb = p1s("rsb", [128, 2], F32)
        ps = p1.enter_context(nc.psum_tensor("ps_a", [128, 8, 512], F32))

        ops = []
        cnt = {"att": 0, "atv": 0, "atx": 0, "ats": 0, "svh": 0,
               "x": 0, "v2s": 0, "nrm": 0, "y1tm": 0, "fmh": 0}

        def op(eng, fn):
            ops.append((eng, fn))

        def t_inc():
            cnt["att"] += 1

        def g_load(fn):
            nx = cnt["atx"]
            op("g", lambda g, nx=nx: g.wait_ge(S["atx"], 16 * nx))
            op("g", fn)
            cnt["ats"] += 1
            nl = cnt["ats"]
            op("g", lambda g, nl=nl: g.wait_ge(S["ats"], 16 * nl))

        def y_wait_loads():
            nl = cnt["ats"]
            nx2 = cnt["x"]
            op("y", lambda sy, nl=nl: sy.wait_ge(S["ats"], 16 * nl))
            op("y", lambda sy, nx2=nx2: sy.wait_ge(S["x"], 16 * nx2))

        def inner(set_, real_in, sr, si, dr, di):
            Er = C["Ein%sr" % set_]
            Ei = C["Ein%si" % set_]
            En = C["Ein%sn" % set_]
            evac_done = {}
            E0 = cnt["atv"]
            for n in range(16):
                sl = slice(n * 512, (n + 1) * 512)
                b = 2 * (n % 4)
                if n >= 4:
                    need = evac_done[n - 4]
                    op("t", lambda t, need=need: t.wait_ge(S["atv"], need))
                for u in (0, 1):
                    pp = slice(64 * u, 64 * u + 64)
                    tp = (64 * u, 64 * u)
                    last = (u == 1)
                    if real_in:
                        op("t", lambda t, pp=pp, tp=tp, b=b, sl=sl, sr=sr,
                           Er=Er:
                           t.matmul(ps[pp, b, :], Er[pp], sr[pp, sl],
                                    start=True, stop=True, tile_position=tp))
                        op("t", lambda t, pp=pp, tp=tp, b=b, sl=sl, sr=sr,
                           Ei=Ei, last=last:
                           t.matmul(ps[pp, b + 1, :], Ei[pp], sr[pp, sl],
                                    start=True, stop=True, tile_position=tp
                                    ).then_inc(S["att"], 1) if last else
                           t.matmul(ps[pp, b + 1, :], Ei[pp], sr[pp, sl],
                                    start=True, stop=True, tile_position=tp))
                    else:
                        op("t", lambda t, pp=pp, tp=tp, b=b, sl=sl, sr=sr,
                           Er=Er:
                           t.matmul(ps[pp, b, :], Er[pp], sr[pp, sl],
                                    start=True, stop=False, tile_position=tp))
                        op("t", lambda t, pp=pp, tp=tp, b=b, sl=sl, si=si,
                           En=En:
                           t.matmul(ps[pp, b, :], En[pp], si[pp, sl],
                                    start=False, stop=True, tile_position=tp))
                        op("t", lambda t, pp=pp, tp=tp, b=b, sl=sl, sr=sr,
                           Ei=Ei:
                           t.matmul(ps[pp, b + 1, :], Ei[pp], sr[pp, sl],
                                    start=True, stop=False, tile_position=tp))
                        op("t", lambda t, pp=pp, tp=tp, b=b, sl=sl, si=si,
                           Er=Er, last=last:
                           t.matmul(ps[pp, b + 1, :], Er[pp], si[pp, sl],
                                    start=False, stop=True, tile_position=tp
                                    ).then_inc(S["att"], 1) if last else
                           t.matmul(ps[pp, b + 1, :], Er[pp], si[pp, sl],
                                    start=False, stop=True, tile_position=tp))
                t_inc()
                nmm = cnt["att"]
                op("v", lambda v, nmm=nmm: v.wait_ge(S["att"], nmm))
                op("v", lambda v, b=b, sl=sl, dr=dr:
                   v.tensor_copy(dr[:, sl], ps[:, b, :]))
                op("s", lambda s, nmm=nmm: s.wait_ge(S["att"], nmm))
                op("s", lambda s, b=b, sl=sl, di=di:
                   s.copy(di[:, sl], ps[:, b + 1, :]))
                if n % 2 == 1 or n == 15:
                    k2 = 2 if (n % 2 == 1) else 1
                    op("v", lambda v, k2=k2: v.drain().then_inc(S["atv"], k2))
                    op("s", lambda s, k2=k2: s.drain().then_inc(S["atv"], k2))
                    cnt["atv"] += 2 * k2
                    for j in range(max(0, n - 1), n + 1):
                        evac_done[j] = cnt["atv"]
        def xbar_pl(src, dst):
            nev = cnt["atv"]
            op("y", lambda sy, nev=nev: sy.wait_ge(S["atv"], nev))
            y_wait_loads()
            for s4 in range(4):
                op("y", lambda sy, s4=s4, src=src, dst=dst: sy.dma_start(
                    dst[:].rearrange("a (t s q) -> a t s q", s=4, q=32)
                    [:, :, s4, :],
                    src[32 * s4:32 * (s4 + 1), :],
                    transpose=True).then_inc(S["atx"], 16))
            cnt["atx"] += 4
            nxb = cnt["atx"]
            op("y", lambda sy, nxb=nxb: sy.wait_ge(S["atx"], 16 * nxb))

        def outer(set_, re_out, sr, si, dr, di, special=False):
            MR = C["M%sR" % set_]
            MI = C.get("M%sI" % set_)
            MN = C["M%sN" % set_]
            nxb = cnt["atx"]
            op("t", lambda t, nxb=nxb: t.wait_ge(S["atx"], 16 * nxb))
            evac_done = {}
            for c in range(64):
                b0 = 2 * (c % 4)
                if c >= 4:
                    need = evac_done[c - 4]
                    op("t", lambda t, need=need: t.wait_ge(S["atv"], need))
                cs, cq = c // 32, c % 32
                rr = sr[:].rearrange("a (t s q) -> a t s q", s=4, q=32)
                ri = si[:].rearrange("a (t s q) -> a t s q", s=4, q=32)
                rhs_r = rr[:, :, cs::2, cq]
                rhs_i = ri[:, :, cs::2, cq]
                for u in (0, 1):
                    pp = slice(64 * u, 64 * u + 64)
                    tp = (64 * u, 64 * u)
                    op("t", lambda t, pp=pp, tp=tp, b0=b0, c=c, rhs_r=rhs_r,
                       MR=MR:
                       t.matmul(ps[pp, b0, 0:128], MR[pp, c], rhs_r[pp],
                                start=True, stop=False, tile_position=tp))
                    op("t", lambda t, pp=pp, tp=tp, b0=b0, c=c, rhs_i=rhs_i,
                       MN=MN, last=(re_out and u == 1):
                       t.matmul(ps[pp, b0, 0:128], MN[pp, c], rhs_i[pp],
                                start=False, stop=True, tile_position=tp
                                ).then_inc(S["att"], 1) if last else
                       t.matmul(ps[pp, b0, 0:128], MN[pp, c], rhs_i[pp],
                                start=False, stop=True, tile_position=tp))
                    if not re_out:
                        op("t", lambda t, pp=pp, tp=tp, b0=b0, c=c,
                           rhs_r=rhs_r, MI=MI:
                           t.matmul(ps[pp, b0 + 1, 0:128], MI[pp, c],
                                    rhs_r[pp], start=True, stop=False,
                                    tile_position=tp))
                        op("t", lambda t, pp=pp, tp=tp, b0=b0, c=c,
                           rhs_i=rhs_i, MR=MR, last=(u == 1):
                           t.matmul(ps[pp, b0 + 1, 0:128], MR[pp, c],
                                    rhs_i[pp], start=False, stop=True,
                                    tile_position=tp
                                    ).then_inc(S["att"], 1) if last else
                           t.matmul(ps[pp, b0 + 1, 0:128], MR[pp, c],
                                    rhs_i[pp], start=False, stop=True,
                                    tile_position=tp))
                t_inc()
                if c % 2 == 1:
                    nmm = cnt["att"]
                    b0e = 2 * ((c - 1) % 4)
                    pvr = ps[:, b0e:b0e + 3:2, 0:128].rearrange(
                        "a b (t ui) -> a b t ui", ui=2)
                    if special:
                        dvv = dr[:].rearrange("a (c ui t) -> a c t ui",
                                              c=64, ui=2)[:, c - 1:c + 1]
                    else:
                        dvv = dr[:].rearrange("a (ui t c) -> a c t ui",
                                              ui=2, c=64)[:, c - 1:c + 1]
                    if re_out and ((c // 2) % 2 == 1):
                        op("s", lambda s, nmm=nmm: s.wait_ge(S["att"], nmm))
                        op("s", lambda s, dvv=dvv, pvr=pvr:
                           s.copy(dvv, pvr))
                        cnt["svh"] += 1
                        op("s", lambda s: s.drain().then_inc(S["svh"], 1))
                        op("v", lambda v, k=cnt["svh"]:
                           v.wait_ge(S["svh"], k))
                        op("v", lambda v: v.nop().then_inc(S["atv"], 1))
                        cnt["atv"] += 1
                    else:
                        op("v", lambda v, nmm=nmm: v.wait_ge(S["att"], nmm))
                        op("v", lambda v, dvv=dvv, pvr=pvr:
                           v.tensor_copy(dvv, pvr))
                        op("v", lambda v: v.drain().then_inc(S["atv"], 1))
                        cnt["atv"] += 1
                    if not re_out:
                        pvi = ps[:, b0e + 1:b0e + 4:2, 0:128].rearrange(
                            "a b (t ui) -> a b t ui", ui=2)
                        dvi = di[:].rearrange("a (ui t c) -> a c t ui",
                                              ui=2, c=64)[:, c - 1:c + 1]
                        op("s", lambda s, nmm=nmm: s.wait_ge(S["att"], nmm))
                        op("s", lambda s, dvi=dvi, pvi=pvi:
                           s.copy(dvi, pvi))
                        op("s", lambda s: s.drain().then_inc(S["atv"], 1))
                        cnt["atv"] += 1
                    evac_done[c - 1] = cnt["atv"]
                    evac_done[c] = cnt["atv"]
        def vdrain_inc():
            op("v", lambda v: v.drain().then_inc(S["atv"], 1))
            cnt["atv"] += 1

        for bk in range(NBK):
            g_load(lambda g, bk=bk: g.dma_start(
                xr[:], xcm.ap()[:, bk * 8192:(bk + 1) * 8192]
            ).then_inc(S["ats"], 16))
            nld = cnt["ats"]
            op("t", lambda t, nld=nld: t.wait_ge(S["ats"], 16 * nld))
            inner("f", True, xr, None, arp, aip)
            xbar_pl(arp, brp)
            xbar_pl(aip, bip)
            outer("f", False, brp, bip, wrp, wip)
            nevS = cnt["atv"]
            op("y", lambda sy, nevS=nevS: sy.wait_ge(S["atv"], nevS))
            op("y", lambda sy: sy.dma_start(xspill.ap()[0], wrp[:]
                                            ).then_inc(S["sxw"], 16))
            op("y", lambda sy: sy.dma_start(xspill.ap()[1], wip[:]
                                            ).then_inc(S["sxw"], 16))
            cnt.setdefault("sxw", 0)
            cnt["sxw"] = cnt.get("sxw", 0) + 2
            nxs = cnt["sxw"]
            op("y", lambda sy, nxs=nxs: sy.wait_ge(S["sxw"], 16 * nxs))
            op("s", lambda s: s.activation(aip[:], wip[:], AF.Square))
            cnt["svh"] += 1
            op("s", lambda s: s.drain().then_inc(S["svh"], 1))
            op("v", lambda v: v.tensor_tensor(arp[:], wrp[:], wrp[:],
                                              ALU.mult))
            op("v", lambda v: v.drain())
            op("v", lambda v, k=cnt["svh"]: v.wait_ge(S["svh"], k))
            op("v", lambda v: v.tensor_tensor(xr[:], arp[:], aip[:],
                                              ALU.add))
            vdrain_inc()
            op("t", lambda t, n=cnt["atv"]: t.wait_ge(S["atv"], n))
            inner("v", True, xr, None, arp, aip)
            xbar_pl(arp, brp)
            xbar_pl(aip, bip)
            outer("v", True, brp, bip, wrp, None)
            op("t", lambda t, n=cnt["atv"]: t.wait_ge(S["atv"], n))
            corr_l0 = wrp[:].rearrange("a (m w) -> a m w", w=64)[:, :, 0]
            op("t", lambda t, corr_l0=corr_l0: t.matmul(
                ps[0:2, 0, 0:128], C["e2"][:], corr_l0,
                start=True, stop=True))
            op("t", lambda t: t.drain().then_inc(S["att"], 1)); t_inc()
            op("v", lambda v, n=cnt["att"]: v.wait_ge(S["att"], n))
            op("v", lambda v: v.tensor_copy(c0e[:], ps[0:2, 0, 0:128]))
            vdrain_inc()
            op("t", lambda t, n=cnt["atv"]: t.wait_ge(S["atv"], n))
            op("t", lambda t: t.matmul(ps[:, 1, 0:128], C["selu"][:],
                                       c0e[:], start=True, stop=True))
            op("t", lambda t: t.drain().then_inc(S["att"], 1)); t_inc()
            op("v", lambda v, n=cnt["att"]: v.wait_ge(S["att"], n))
            op("v", lambda v: v.tensor_copy(c0bc[:], ps[:, 1, 0:128]))
            op("v", lambda v: v.drain())
            op("v", lambda v: v.tensor_tensor(
                arp[:].rearrange("a (m w) -> a m w", w=64),
                wrp[:].rearrange("a (m w) -> a m w", w=64),
                c0bc[:].rearrange("a (m o) -> a m o", o=1
                                  ).broadcast_to((128, 128, 64)),
                ALU.subtract))
            vdrain_inc()
            op("s", lambda s, n=cnt["atv"]: s.wait_ge(S["atv"], n))
            op("s", lambda s: s.activation(xr[:], arp[:], AF.Exp))
            op("s", lambda s: s.drain().then_inc(S["v2s"], 1))
            cnt["v2s"] += 1
            op("v", lambda v, n=cnt["v2s"]: v.wait_ge(S["v2s"], n))
            op("v", lambda v: v.reduce_sum(
                s1[:], xr[:].rearrange("a (m w) -> a m w", w=64),
                axis=AX.X))
            op("v", lambda v: v.drain())
            op("v", lambda v: v.tensor_copy(s1b[:], s1[:]))
            vdrain_inc()
            op("t", lambda t, n=cnt["atv"]: t.wait_ge(S["atv"], n))
            op("t", lambda t: t.matmul(ps[0:2, 2, 0:128], C["hs"][:], s1b[:],
                                       start=True, stop=True))
            op("t", lambda t: t.drain().then_inc(S["att"], 1)); t_inc()
            op("v", lambda v, n=cnt["att"]: v.wait_ge(S["att"], n))
            op("v", lambda v: v.tensor_copy(zE[:], ps[0:2, 2, 0:128]))
            op("v", lambda v: v.drain())
            op("v", lambda v: v.reciprocal(zRf[:], zE[:]))
            op("v", lambda v: v.drain())
            op("v", lambda v: v.tensor_copy(zR[:], zRf[:]))
            vdrain_inc()
            op("t", lambda t, n=cnt["atv"]: t.wait_ge(S["atv"], n))
            op("t", lambda t: t.matmul(ps[:, 3, 0:128], C["selu"][:], zR[:],
                                       start=True, stop=True))
            op("t", lambda t: t.drain().then_inc(S["att"], 1)); t_inc()
            op("v", lambda v, n=cnt["att"]: v.wait_ge(S["att"], n))
            op("v", lambda v: v.tensor_copy(zbc[:], ps[:, 3, 0:128]))
            op("v", lambda v: v.drain())
            op("v", lambda v: v.tensor_tensor(
                xr[:].rearrange("a (m w) -> a m w", w=64),
                xr[:].rearrange("a (m w) -> a m w", w=64),
                zbc[:].rearrange("a (m o) -> a m o", o=1
                                 ).broadcast_to((128, 128, 64)),
                ALU.mult))
            op("v", lambda v: v.drain())
            wl0a = xr[:].rearrange("a (m w) -> a m w", w=64)[0:1, :, 0]
            wl0b = xr[:].rearrange("a (m w) -> a m w", w=64)[64:65, :, 0]
            op("v", lambda v, wl0a=wl0a: v.tensor_scalar_add(wl0a, wl0a, -1.0))
            op("v", lambda v, wl0b=wl0b: v.tensor_scalar_add(wl0b, wl0b, -1.0))
            vdrain_inc()
            op("t", lambda t, n=cnt["atv"]: t.wait_ge(S["atv"], n))
            inner("f", True, xr, None, arp, aip)
            xbar_pl(arp, brp)
            xbar_pl(aip, bip)
            outer("f", False, brp, bip, wrp, wip)
            g_load(lambda g: g.dma_start(arp[:], xspill.ap()[0]
                                         ).then_inc(S["ats"], 16))
            g_load(lambda g: g.dma_start(aip[:], xspill.ap()[1]
                                         ).then_inc(S["ats"], 16))
            nXr = cnt["ats"]
            op("v", lambda v, nXr=nXr: v.wait_ge(S["ats"], 16 * nXr))
            op("v", lambda v: v.tensor_tensor(brp[:], arp[:], wrp[:],
                                              ALU.mult))
            op("v", lambda v: v.tensor_tensor(bip[:], aip[:], wip[:],
                                              ALU.mult))
            op("v", lambda v: v.drain())
            op("v", lambda v: v.tensor_tensor(brp[:], brp[:], bip[:],
                                              ALU.add))
            op("v", lambda v: v.tensor_tensor(bip[:], aip[:], wrp[:],
                                              ALU.mult))
            op("v", lambda v: v.drain())
            op("v", lambda v: v.tensor_tensor(arp[:], arp[:], wip[:],
                                              ALU.mult))
            op("v", lambda v: v.drain())
            op("v", lambda v: v.tensor_tensor(bip[:], bip[:], arp[:],
                                              ALU.subtract))
            vdrain_inc()
            op("t", lambda t, n=cnt["atv"]: t.wait_ge(S["atv"], n))
            inner("v", False, brp, bip, arp, aip)
            xbar_pl(arp, wrp)
            xbar_pl(aip, wip)
            outer("v", True, wrp, wip, ctb, None, special=True)
            nevC = cnt["atv"]
            op("y", lambda sy, nevC=nevC: sy.wait_ge(S["atv"], nevC))

            for u4 in range(2):
                tc = bk * 2 + u4
                for st2 in range(2):
                    op("y", lambda sy, u4=u4, st2=st2: sy.dma_start(
                        cttm[:].rearrange("a (c iq) -> a c iq", iq=64)
                        [:, :, st2 * 32:(st2 + 1) * 32],
                        ctb[64 * u4 + 32 * st2:
                            64 * u4 + 32 * (st2 + 1), :],
                        transpose=True).then_inc(S["atx"], 16))
                    cnt["atx"] += 1
                nxc = cnt["atx"]
                op("y", lambda sy, nxc=nxc: sy.wait_ge(S["atx"], 16 * nxc))
                g_load(lambda g, tc=tc: g.dma_start(
                    xtb[:, 0], xtm.ap()[tc * 128:(tc + 1) * 128, :]
                ).then_inc(S["ats"], 16))
                nx = cnt["ats"]
                op("v", lambda v, nx=nx: v.wait_ge(S["ats"], 16 * nx))
                nxc2 = cnt["atx"]
                op("v", lambda v, nxc2=nxc2: v.wait_ge(S["atx"], 16 * nxc2))
                op("v", lambda v, tc=tc, u4=u4: v.scalar_tensor_tensor(
                    vb.rearrange("a (i2 c) -> a i2 c", i2=64),
                    xtb[:, 0].rearrange("a (i2 c) -> a i2 c", i2=64),
                    2.0,
                    cttm[:].rearrange("a (c i2) -> a c i2", i2=64
                                      ).transpose([0, 2, 1]),
                    ALU.mult, ALU.add))
                op("v", lambda v: v.drain())
                for k in range(8):
                    op("v", lambda v, k=k: v.bn_stats(
                        bn6[:, 6 * k:6 * k + 6],
                        vb[:, 512 * k:512 * (k + 1)]))
                op("v", lambda v: v.drain())
                op("v", lambda v: v.bn_aggr(bn2[:], bn6[:]))
                op("v", lambda v: v.drain())
                op("v", lambda v: v.tensor_scalar_add(vtmp[:], bn2[:, 1:2],
                                                      EPS))
                op("v", lambda v: v.drain().then_inc(S["v2s"], 1))
                cnt["v2s"] += 1
                nv = cnt["v2s"]
                op("s", lambda s, nv=nv: s.wait_ge(S["v2s"], nv))
                op("s", lambda s: s.activation(vrec[:], vtmp[:], AF.Ln))
                op("s", lambda s: s.drain())
                op("s", lambda s, tc=tc: s.activation(
                    rsb[:, tc % 2:tc % 2 + 1], vrec[:], AF.Exp,
                    scale=-0.5))
                op("s", lambda s: s.drain().then_inc(S["nrm"], 1))
                cnt["nrm"] += 1
                nn = cnt["nrm"]
                op("v", lambda v, nn=nn: v.wait_ge(S["nrm"], nn))
                nxw = cnt["atx"]
                op("v", lambda v, nxw=nxw: v.wait_ge(S["atx"], 16 * nxw))
                op("v", lambda v, tc=tc: v.tensor_scalar(
                    y1tm[:, 0], vb, bn2[:, 0:1],
                    rsb[:, tc % 2:tc % 2 + 1], ALU.subtract, ALU.mult))
                op("v", lambda v: v.drain().then_inc(S["y1tm"], 1))
                cnt["y1tm"] += 1
                ny = cnt["y1tm"]
                op("y", lambda sy, ny=ny: sy.wait_ge(S["y1tm"], ny))
                y_wait_loads()
                for s4 in range(4):
                    op("y", lambda sy, s4=s4, tc=tc: sy.dma_start(
                        y1f[:, 0, :, 32 * s4:32 * (s4 + 1)],
                        y1tm[32 * s4:32 * (s4 + 1), 0],
                        transpose=True).then_inc(S["atx"], 16))
                    cnt["atx"] += 1
                nfy = cnt["atx"]
                op("y", lambda sy, nfy=nfy: sy.wait_ge(S["atx"], 16 * nfy))
                op("y", lambda sy, tc=tc: sy.dma_start(
                    y1hbm.ap()[:, :, tc * 128:(tc + 1) * 128],
                    y1f[:, 0]).then_inc(S["fmh"], 16))
                cnt["fmh"] += 1
                nfh = cnt["fmh"]
                op("y", lambda sy, nfh=nfh: sy.wait_ge(S["fmh"], 16 * nfh))

        with nc.Block() as blk:
            @blk.tensor
            def _(t):
                for e, fn in ops:
                    if e == "t":
                        fn(t)

            @blk.vector
            def _(v):
                for e, fn in ops:
                    if e == "v":
                        fn(v)

            @blk.scalar
            def _(s):
                for e, fn in ops:
                    if e == "s":
                        fn(s)

            @blk.sync
            def _(sy):
                for e, fn in ops:
                    if e == "y":
                        fn(sy)

            @blk.gpsimd
            def _(g):
                for e, fn in ops:
                    if e == "g":
                        fn(g)

    with contextlib.ExitStack() as p2:
        p2s = lambda n, s, d: p2.enter_context(nc.sbuf_tensor(n, s, d))
        NW = 4
        ident_b = p2s("ident_b", [128, 128], BF16)
        ident_f = p2s("ident_f", [128, 128], F32)
        eop = p2s("eop", [8, 128], BF16)
        one1 = p2s("one1", [1, 128], BF16)
        wr_sb = p2s("wr_sb", [128, NDC, 8], BF16)
        acat_sb = p2s("acat_sb", [128, NDC, 128], BF16)
        bcat_sb = p2s("bcat_sb", [128, D], BF16)
        bb_sb = p2s("bb_sb", [1, D], BF16)
        y1sb = p2s("y1sb", [128, NDC, 1024], BF16)
        wt = p2s("wt", [128, 4, NW * 512], BF16)
        stg = p2s("stg", [128, 4, 512], BF16)
        lg_sb = p2s("lg_sb", [8, 1024], F32)
        ltm = p2s("ltm", [128, TCSB, 8], F32)
        rmx = p2s("rmx", [128, TCSB, 1], F32)
        rtmp = p2s("rtmp", [128, TCSB, 1], F32)
        prb = p2s("prb", [128, TCSB, 8], F32)
        pm1 = p2s("pm1", [128, TCSB, 1], F32)
        pmsk = p2s("pmsk", [128, TCSB, 8], F32)
        ptmp = p2s("ptmp", [128, TCSB, 8], F32)
        pm2 = p2s("pm2", [128, TCSB, 1], F32)
        wrtw = p2s("wrtw", [128, TCSB, 8], F32)
        wT_sb = p2s("wT_sb", [8, 1024], BF16)
        h_sb = p2s("h_sb", [128, 1024], F32)
        hp_sb = p2s("hp_sb", [128, 1024], BF16)
        stats = p2s("stats", [128, TCSB, NOB, 6], F32)
        m2 = p2s("m2", [128, TCSB, 2], F32)
        vrec2 = p2s("vrec2", [128, TCSB], F32)
        lt2 = p2s("lt2", [128, 1], F32)
        rs2 = p2s("rs2", [128, NTC], F32)
        nmrs = p2s("nmrs", [128, NTC], F32)
        rlb = p2s("rlb", [128, 4, 512], BF16)
        outb = p2s("outb", [128, 4, 512], F32)
        psum_a = p2.enter_context(nc.psum_tensor("psum_a", [128, 8, 512],
                                                 F32))
        WDC = NDC // NW

        with nc.Block() as blk:
            @blk.sync
            def _(sy2):
                sy2.dma_start(ident_b[:], identb.ap()).then_inc(S["c_ld"], 16)
                sy2.dma_start(ident_f[:], identf.ap()).then_inc(S["c_ld"], 16)
                sy2.dma_start(eop[:], eopat.ap()).then_inc(S["c_ld"], 16)
                sy2.dma_start(one1[:], ones1.ap()).then_inc(S["c_ld"], 16)

            @blk.gpsimd
            def _(g):
                g.dma_start(wr_sb[:],
                            wrT.ap().rearrange("(dc p) e -> p dc e", p=128)
                            ).then_inc(S["c_ld2"], 16)
                g.dma_start(acat_sb[:],
                            acatT.ap().rearrange("(dc p) e -> p dc e", p=128)
                            ).then_inc(S["c_ld2"], 16)
                g.dma_start(bcat_sb[:], bcat.ap()).then_inc(S["c_ld2"], 16)
                g.dma_start(bb_sb[0:1, :],
                            bb.ap().rearrange("(a d) -> a d", a=1)
                            ).then_inc(S["c_ld2"], 16)
                k = 0
                for sb in range(NSB):
                    if sb > 0:
                        g.wait_ge(S["ps"], NOB * sb)
                    g.dma_start(y1sb[:],
                                y1hbm.ap()[:, :, sb * 1024:(sb + 1) * 1024]
                                ).then_inc(S["y1l"], 16)
                    for ob in range(NOB):
                        for wq in range(WDC):
                            if k >= 4:
                                g.wait_ge(S["wfree"], k - 3)
                            g.dma_start(
                                wt[:, k % 4].rearrange("p (a b) -> p a b",
                                                       a=NW),
                                wbT.ap().rearrange("(dc p) o -> p dc o",
                                                   p=128)
                                [:, wq * NW:(wq + 1) * NW,
                                 ob * 512:(ob + 1) * 512],
                            ).then_inc(S["wt"], 16)
                            k += 1

            @blk.tensor
            def _(t):
                k = 0
                nevac = 0
                for sb in range(NSB):
                    if sb == 0:
                        t.wait_ge(S["c_ld"], 16 * (len(C) + 4))
                        t.wait_ge(S["c_ld2"], 16 * 4)
                    t.wait_ge(S["y1l"], 16 * (sb + 1))
                    if sb > 0:
                        t.wait_ge(S["stg"], 64 * sb)
                    for half in range(2):
                        for dc in range(NDC):
                            t.matmul(psum_a[0:8, half, :], wr_sb[:, dc],
                                     y1sb[:, dc,
                                          512 * half:512 * (half + 1)],
                                     start=(dc == 0), stop=(dc == NDC - 1))
                    t.drain().then_inc(S["lgmm"], 1)
                    for half in range(2):
                        for dc in range(NDC):
                            t.matmul(psum_a[:, 2 + half, :], acat_sb[:, dc],
                                     y1sb[:, dc,
                                          512 * half:512 * (half + 1)],
                                     start=(dc == 0), stop=(dc == NDC - 1))
                    t.drain().then_inc(S["hmm"], 1)
                    t.wait_ge(S["lgev"], sb + 1)
                    for q in range(TCSB):
                        t.transpose(psum_a[:, 4, q * 8:(q + 1) * 8],
                                    lg_sb[:, q * 128:(q + 1) * 128],
                                    ident_f[0:8, 0:8])
                    t.drain().then_inc(S["ltT"], 1)
                    t.wait_ge(S["wrtv"], sb + 1)
                    for q in range(TCSB):
                        t.transpose(psum_a[0:8, 4 + q // 4,
                                           (q % 4) * 128:(q % 4) * 128 + 128],
                                    wrtw[:, q], ident_f[:])
                    t.drain().then_inc(S["wrtT"], 1)
                    t.wait_ge(S["wTev"], sb + 1)
                    for half in range(2):
                        t.matmul(psum_a[:, 6 + half, :], eop[:],
                                 wT_sb[:, 512 * half:512 * (half + 1)],
                                 start=True, stop=True)
                    t.drain().then_inc(S["wbc"], 1)
                    t.wait_ge(S["hp"], sb + 1)
                    for ob in range(NOB):
                        for dc in range(NDC):
                            if dc % NW == 0:
                                t.wait_ge(S["wt"], 16 * (k + 1))
                            wslice = wt[:, k % 4, (dc % NW) * 512:
                                        (dc % NW) * 512 + 512]
                            for q in range(TCSB):
                                if ob >= 1 and dc == 0:
                                    t.wait_ge(S["stg"],
                                              nevac - TCSB + q + 1)
                                last_mm = t.matmul(
                                    psum_a[:, q, :],
                                    y1sb[:, dc, q * 128:(q + 1) * 128],
                                    wslice, start=(dc == 0), stop=False)
                                if ob * 512 <= dc * 128 < (ob + 1) * 512:
                                    off = dc * 128 - ob * 512
                                    t.matmul(psum_a[:, q, off:off + 128],
                                             y1sb[:, dc,
                                                  q * 128:(q + 1) * 128],
                                             ident_b[:], start=False,
                                             stop=False)
                            if dc % NW == NW - 1:
                                last_mm.then_inc(S["wfree"], 1)
                                k += 1
                        for q in range(TCSB):
                            t.matmul(psum_a[:, q, :],
                                     hp_sb[:, q * 128:(q + 1) * 128],
                                     bcat_sb[:, ob * 512:(ob + 1) * 512],
                                     start=False, stop=False)
                            t.matmul(psum_a[:, q, :], one1[:],
                                     bb_sb[:, ob * 512:(ob + 1) * 512],
                                     start=False, stop=True)
                        t.drain().then_inc(S["ps"], 1)
                        nevac += TCSB

            @blk.vector
            def _(v):
                nps = 0
                nevac = 0
                for sb in range(NSB):
                    v.wait_ge(S["lgmm"], sb + 1)
                    v.tensor_copy(lg_sb[:],
                                  psum_a[0:8, 0:2, :].rearrange(
                                      "p a b -> p (a b)"))
                    v.drain().then_inc(S["lgev"], 1)
                    v.wait_ge(S["ltT"], sb + 1)
                    v.tensor_copy(ltm[:],
                                  psum_a[:, 4, 0:TCSB * 8].rearrange(
                                      "p (a b) -> p a b", b=8))
                    v.drain()
                    v.reduce_max(rmx[:], ltm[:], axis=AX.X)
                    v.drain()
                    v.tensor_tensor(prb[:], ltm[:],
                                    rmx[:].broadcast_to((128, TCSB, 8)),
                                    ALU.subtract)
                    v.drain().then_inc(S["exp1"], 1)
                    v.wait_ge(S["exp2"], sb + 1)
                    v.reduce_max(pm1[:], prb[:], axis=AX.X)
                    v.drain()
                    v.tensor_tensor(pmsk[:], prb[:],
                                    pm1[:].broadcast_to((128, TCSB, 8)),
                                    ALU.is_equal)
                    v.drain()
                    v.tensor_tensor(ptmp[:], prb[:], pmsk[:], ALU.mult)
                    v.drain()
                    v.tensor_tensor(ptmp[:], prb[:], ptmp[:], ALU.subtract)
                    v.drain()
                    v.reduce_max(pm2[:], ptmp[:], axis=AX.X)
                    v.drain()
                    v.tensor_tensor(pmsk[:], prb[:],
                                    pm2[:].broadcast_to((128, TCSB, 8)),
                                    ALU.is_ge)
                    v.drain()
                    v.tensor_tensor(wrtw[:], prb[:], pmsk[:], ALU.mult)
                    v.tensor_add(rtmp[:], pm1[:], pm2[:])
                    v.drain()
                    v.reciprocal(pm1[:], rtmp[:])
                    v.drain()
                    v.tensor_scalar_mul(pm1[:], pm1[:], 2.0)
                    v.drain()
                    v.tensor_tensor(wrtw[:], wrtw[:],
                                    pm1[:].broadcast_to((128, TCSB, 8)),
                                    ALU.mult)
                    v.drain().then_inc(S["wrtv"], 1)
                    v.wait_ge(S["wrtT"], sb + 1)
                    v.tensor_copy(wT_sb[:],
                                  psum_a[0:8, 4:6, :].rearrange(
                                      "p a b -> p (a b)"))
                    v.drain().then_inc(S["wTev"], 1)
                    v.wait_ge(S["hmm"], sb + 1)
                    v.wait_ge(S["wbc"], sb + 1)
                    v.tensor_copy(h_sb[:],
                                  psum_a[:, 2:4, :].rearrange(
                                      "p a b -> p (a b)"))
                    v.drain()
                    v.tensor_tensor(hp_sb[:, 0:512], h_sb[:, 0:512],
                                    psum_a[:, 6, :], ALU.mult)
                    v.tensor_tensor(hp_sb[:, 512:1024], h_sb[:, 512:1024],
                                    psum_a[:, 7, :], ALU.mult)
                    v.drain().then_inc(S["hp"], 1)
                    for ob in range(NOB):
                        nps += 1
                        v.wait_ge(S["ps"], nps)
                        for q in range(TCSB):
                            v.bn_stats(stats[:, q, ob, :], psum_a[:, q, :])
                            if nevac >= 4:
                                v.wait_ge(S["ffst"], 16 * (nevac - 3))
                            v.tensor_copy(stg[:, nevac % 4], psum_a[:, q, :])
                            v.drain().then_inc(S["stg"], 1)
                            nevac += 1
                    v.drain()
                    for q in range(TCSB):
                        tc = sb * TCSB + q
                        v.bn_aggr(m2[:, q],
                                  stats[:, q].rearrange("p a b -> p (a b)"))
                        v.drain()
                        v.tensor_scalar_add(vrec2[:, q:q + 1],
                                            m2[:, q, 1:2], EPS)
                        v.drain().then_inc(S["ln2b"], 1)
                        v.wait_ge(S["nmrs"], tc + 1)
                        v.tensor_scalar(nmrs[:, tc:tc + 1], m2[:, q, 0:1],
                                        rs2[:, tc:tc + 1], -1.0,
                                        ALU.mult, ALU.mult)
                        v.drain()
                    for q in range(TCSB):
                        tc = sb * TCSB + q
                        for f in range(8):
                            idx = tc * 8 + f
                            v.wait_ge(S["rl2"],
                                      16 * min(idx + 3, 64 * (sb + 1)))
                            if idx >= 4:
                                v.wait_ge(S["outd"], 16 * (idx - 3))
                            v.tensor_scalar(outb[:, idx % 4],
                                            rlb[:, idx % 4],
                                            rs2[:, tc:tc + 1],
                                            nmrs[:, tc:tc + 1],
                                            ALU.mult, ALU.add)
                            v.drain().then_inc(S["nrmd"], 1)

            @blk.scalar
            def _(s):
                for sb in range(NSB):
                    s.wait_ge(S["exp1"], sb + 1)
                    s.activation(prb[:], prb[:], AF.Exp)
                    s.drain().then_inc(S["exp2"], 1)
                    for q in range(TCSB):
                        tc = sb * TCSB + q
                        s.wait_ge(S["ln2b"], tc + 1)
                        s.activation(lt2[:], vrec2[:, q:q + 1], AF.Ln)
                        s.drain()
                        s.activation(rs2[:, tc:tc + 1], lt2[:],
                                     AF.Exp, scale=-0.5)
                        s.drain().then_inc(S["nmrs"], 1)

            @blk.sync
            def _(sy):
                nst = 0
                for sb in range(NSB):
                    for ob in range(NOB):
                        for q in range(TCSB):
                            tc = sb * TCSB + q
                            sy.wait_ge(S["stg"], nst + 1)
                            sy.dma_start(
                                ffnout.ap()[tc * 128:(tc + 1) * 128,
                                            ob * 512:(ob + 1) * 512],
                                stg[:, nst % 4],
                            ).then_inc(S["ffst"], 16)
                            nst += 1
                    sy.wait_ge(S["ffst"], 16 * 64 * (sb + 1))
                    for q in range(TCSB):
                        tc = sb * TCSB + q
                        for f in range(8):
                            idx = tc * 8 + f
                            if idx >= 4:
                                sy.wait_ge(S["nrmd"], idx - 3)
                            sy.dma_start(
                                rlb[:, idx % 4],
                                ffnout.ap()[tc * 128:(tc + 1) * 128,
                                            f * 512:(f + 1) * 512]
                            ).then_inc(S["rl2"], 16)
                            if idx >= 4:
                                pidx = idx - 4
                                sy.wait_ge(S["nrmd"], pidx + 2)
                                ptc, pf = pidx // 8, pidx % 8
                                sy.dma_start(
                                    out.ap()[ptc * 128:(ptc + 1) * 128,
                                             pf * 512:(pf + 1) * 512],
                                    outb[:, pidx % 4],
                                ).then_inc(S["outd"], 16)
                for pidx in range(NTC * 8 - 4, NTC * 8):
                    sy.wait_ge(S["nrmd"], min(pidx + 2, NTC * 8))
                    ptc, pf = pidx // 8, pidx % 8
                    sy.dma_start(out.ap()[ptc * 128:(ptc + 1) * 128,
                                          pf * 512:(pf + 1) * 512],
                                 outb[:, pidx % 4]).then_inc(S["outd"], 16)
                sy.wait_ge(S["outd"], 16 * NTC * 8)

    stp.close()
    return nc


def kernel(L0_attn_x, L0_ffn_W_base, L0_ffn_b_base, L0_ffn_W_router,
           L0_ffn_lora_A, L0_ffn_lora_B):
    xf = np.ascontiguousarray(
        np.asarray(L0_attn_x, dtype=np.float32).reshape(16384, D))
    wbT_h = np.ascontiguousarray(np.asarray(L0_ffn_W_base, np.float32).T)
    bb_h = np.ascontiguousarray(np.asarray(L0_ffn_b_base, np.float32))
    wrT_h = np.ascontiguousarray(np.asarray(L0_ffn_W_router, np.float32).T)
    acatT_h = np.ascontiguousarray(
        np.asarray(L0_ffn_lora_A, np.float32).reshape(128, D).T)
    bcat_h = np.ascontiguousarray(
        np.asarray(L0_ffn_lora_B, np.float32).transpose(0, 2, 1)
        .reshape(128, D))
    identb_h = np.eye(128, dtype=NB)
    identf_h = np.eye(128, dtype=np.float32)
    eopat_h = (np.arange(128)[None, :] // 16 ==
               np.arange(8)[:, None]).astype(NB)
    ones1_h = np.ones((1, 128), dtype=NB)
    consts_h = host_constants()

    nc = build_nc()
    in_maps = []
    for c in range(8):
        xs = xf[c * R:(c + 1) * R]
        xcm_h = np.ascontiguousarray(
            xs.reshape(NBK, 128, 2, 64, 64).transpose(2, 3, 0, 1, 4)
            .reshape(128, NBK * 8192))
        xtm_h = np.ascontiguousarray(xs[PERM])
        m = dict(xcm=xcm_h, xtm=xtm_h, wbT=wbT_h, bb=bb_h, wrT=wrT_h,
                 acatT=acatT_h, bcat=bcat_h, identb=identb_h,
                 identf=identf_h, eopat=eopat_h, ones1=ones1_h)
        for k, v in consts_h.items():
            m["c_" + k] = v
        in_maps.append(m)
    res = run_bass_kernel_spmd(nc, in_maps, core_ids=list(range(8)))
    outp = np.empty((16384, D), np.float32)
    inv = np.empty(R, np.int64)
    inv[PERM] = np.arange(R)
    for c in range(8):
        outp[c * R:(c + 1) * R] = res.results[c]["out"][inv]
    return outp.reshape(4, 4096, 4096).astype(np.float32)
